# version 02
# baseline (speedup 1.0000x reference)
import numpy as np
from contextlib import ExitStack

import concourse.bass as bass
import concourse.mybir as mybir
from concourse.bass_utils import run_bass_kernel_spmd

F32 = mybir.dt.float32
BF16 = mybir.dt.bfloat16
AF = mybir.ActivationFunctionType
ALU = mybir.AluOpType
AX = mybir.AxisListType

NB = 2
S = 2048
NK = 8
NPAIR = 8
NF = 11
NE = 10
SCALE = 0.03125
EPS = 1e-5
JBW = 1024
DEBUG = False

ENGS = ("t", "v", "s", "y", "g")


class Plan:
    def __init__(self, nc, sems):
        self.nc = nc
        self.sems = sems
        self.cnt = {k: 0 for k in sems}
        self.q = {e: [] for e in ENGS}
        self.lastwait = {}

    def emit(self, eng, fn):
        self.q[eng].append(fn)
        self.drained = getattr(self, "drained", {})
        self.drained[eng] = False

    def drain(self, eng):
        d = getattr(self, "drained", {})
        if d.get(eng):
            return
        self.q[eng].append(lambda E: E.drain())
        d[eng] = True
        self.drained = d

    def wait(self, eng, sem, val):
        if val is None or val <= 0:
            return
        key = (eng, sem)
        if self.lastwait.get(key, -1) >= val:
            return
        self.lastwait[key] = val
        h = self.sems[sem]
        self.emit(eng, lambda E, h=h, v=val: E.wait_ge(h, v))

    def inc(self, sem, n=1):
        self.cnt[sem] += n
        return self.cnt[sem]


class Queue:
    R = 3

    def __init__(self, plan, eng, semnames):
        assert len(semnames) == self.R
        self.p = plan
        self.eng = eng
        self.names = semnames
        self.iss = [0] * self.R
        self.n = 0

    def batch(self, dmas, gate=None, transpose=False):
        pl = self.p
        r = self.n % self.R
        sem = self.names[r]
        pl.wait(self.eng, sem, 16 * self.iss[r])
        if gate is not None:
            gates = gate if isinstance(gate, list) else [gate]
            for gsem, gval in gates:
                pl.wait(self.eng, gsem, gval)
        h = pl.sems[sem]
        for (o, i) in dmas:
            pl.inc(sem, 16)
            if transpose:
                pl.emit(self.eng, lambda E, o=o, i=i, h=h:
                        E.dma_start_transpose(o, i).then_inc(h, 16))
            else:
                pl.emit(self.eng, lambda E, o=o, i=i, h=h:
                        E.dma_start(out=o, in_=i).then_inc(h, 16))
            self.iss[r] += 1
        self.n += 1
        return (sem, 16 * self.iss[r])

    def wait_tok(self, eng, tok):
        if tok is not None:
            self.p.wait(eng, tok[0], tok[1])

    def flush(self):
        for r in range(self.R):
            self.p.wait(self.eng, self.names[r], 16 * self.iss[r])


def build_program():
    nc = bass.Bass("TRN2", target_bir_lowering=False)

    x_d = nc.dram_tensor("x", [NB, 1024, S], F32, kind="ExternalInput").ap()
    u_d = nc.dram_tensor("u", [1024, 1024], F32, kind="ExternalInput").ap()
    v_d = nc.dram_tensor("v", [1024, 1024], F32, kind="ExternalInput").ap()
    ut_d = nc.dram_tensor("ut", [1024, 1024], F32, kind="ExternalInput").ap()
    vt_d = nc.dram_tensor("vt", [1024, 1024], F32, kind="ExternalInput").ap()
    gwt_d = nc.dram_tensor("gwt", [1024, 8], F32, kind="ExternalInput").ap()
    wg_d = nc.dram_tensor("wg", [8, 1024, 1408], F32, kind="ExternalInput").ap()
    wu_d = nc.dram_tensor("wu", [8, 1024, 1408], F32, kind="ExternalInput").ap()
    wd_d = nc.dram_tensor("wd", [8, 1408, 1024], F32, kind="ExternalInput").ap()
    swg_d = nc.dram_tensor("swg", [1024, 2816], F32, kind="ExternalInput").ap()
    swu_d = nc.dram_tensor("swu", [1024, 2816], F32, kind="ExternalInput").ap()
    swd_d = nc.dram_tensor("swd", [2816, 1024], F32, kind="ExternalInput").ap()
    id_d = nc.dram_tensor("ident", [128, 128], F32, kind="ExternalInput").ap()
    out_d = nc.dram_tensor("out", [NB, 1024, S], F32, kind="ExternalOutput").ap()
    scr_d = nc.dram_tensor("scr", [2, S], F32, kind="Internal").ap()
    scrcb_d = nc.dram_tensor("scrcb", [8, S], F32, kind="Internal").ap()

    dbg = {}
    if DEBUG:
        dbg["xln"] = nc.dram_tensor("d_xln", [NB, 2, NK, 128, S], F32,
                                    kind="ExternalOutput").ap()
        dbg["qz"] = nc.dram_tensor("d_qz", [NB, 2, 128, S], F32,
                                   kind="ExternalOutput").ap()
        dbg["qhv"] = nc.dram_tensor("d_qhv", [NB, 2, NK, 128, S], F32,
                                    kind="ExternalOutput").ap()
        dbg["cb"] = nc.dram_tensor("d_cb", [NB, 8, S], F32,
                                   kind="ExternalOutput").ap()
        dbg["x2"] = nc.dram_tensor("d_x2", [NB, NK, 128, S], F32,
                                   kind="ExternalOutput").ap()

    BASE, TOP = 22944, 229344
    cur = [BASE]

    def sb(name, shape, dtype, at=None):
        nbytes = int(np.prod(shape[1:])) * (2 if dtype == BF16 else 4)
        if at is None:
            at = cur[0]
            cur[0] += (nbytes + 31) & ~31
        assert at + nbytes <= TOP, (name, at, nbytes)
        return nc.alloc_sbuf_tensor_at(name, shape, dtype, offset=at).ap()

    ident_s = sb("ident_s", [128, 128], F32)
    gwt_s = sb("gwt_s", [128, 8, 8], BF16)
    ones_s = sb("ones_s", [128, 1], F32)
    consts = sb("consts", [1, 8], F32)
    xln = sb("xln", [128, NK, S], BF16)
    tmpS = sb("tmpS", [128, JBW], F32)
    tmp2 = sb("tmp2", [128, JBW], F32)
    rt_l = sb("rt_l", [128, 8], F32)
    rt_nm1 = sb("rt_nm1", [128, 1], F32)
    rt_nm2 = sb("rt_nm2", [128, 1], F32)
    rt_p = sb("rt_p", [128, 8], F32)
    rt_t = sb("rt_t", [128, 8], F32)
    rt_e2 = sb("rt_e2", [128, 1], F32)
    rt_w = sb("rt_w", [128, 1], F32)
    combine = sb("combine", [128, 16, 8], F32)

    R0 = cur[0]; cur[0] += 65536
    xres = sb("xres", [128, NK, S], F32, at=R0)
    qzu_r = sb("qzu_r", [128, S], BF16, at=R0)
    qzv_r = sb("qzv_r", [128, S], BF16, at=R0 + 4096)
    u_tok = sb("u_tok", [128, 16, 128], BF16, at=R0 + 8192)
    v_tok = sb("v_tok", [128, 16, 128], BF16, at=R0 + 12288)
    qh_st = sb("qh_st", [128, 2, 1024], BF16, at=R0 + 16384)

    R1 = cur[0]; cur[0] += 65536
    qhv = sb("qhv", [128, 2, NK, S], BF16, at=R1)
    g_st = sb("g_st", [128, NF, JBW], BF16, at=R1)
    wgc = sb("wgc", [128, 2, 8, 128], BF16, at=R1 + 22528)
    wuc = sb("wuc", [128, 2, 8, 128], BF16, at=R1 + 26624)
    yacc = sb("yacc", [128, NK, JBW], F32, at=R1 + 30720)

    R2 = cur[0]; cur[0] += 32768
    ln_m = sb("ln_m", [1, S], F32, at=R2)
    ln_v = sb("ln_v", [1, S], F32, at=R2 + 8192)
    xsq = sb("xsq", [128, S], F32, at=R2 + 16384)
    ln_b = sb("ln_b", [128, 2, JBW], F32, at=R2 + 24576)
    uT_s = sb("uT_s", [128, NK, 1024], BF16, at=R2)
    vT_s = sb("vT_s", [128, NK, 1024], BF16, at=R2 + 16384)
    u_s = sb("u_s", [128, NK, 1024], BF16, at=R2)
    v_s = sb("v_s", [128, NK, 1024], BF16, at=R2 + 16384)
    wdc = sb("wdc", [128, NF, 1024], BF16, at=R2)
    cb_b = sb("cb_b", [128, 2, JBW], F32, at=R2 + 22528)
    cb_f = sb("cb_f", [8, S], F32, at=R2 + 22528)
    assert cur[0] <= TOP, cur[0]

    ps = nc.alloc_psum_tensor("ps", [128, 8, 512], F32).ap()

    def bank(k, w=1):
        if w == 1:
            return ps[:, k, :]
        return ps[:, k:k + w, :].rearrange("p a b -> p (a b)")

    ctx = ExitStack()
    sems = {}
    for name in ("ga", "gb", "gc", "ya", "yb", "yc", "pe", "se", "ve", "fin"):
        sems[name] = ctx.enter_context(nc.semaphore(name))
    pl = Plan(nc, sems)
    QG = Queue(pl, "g", ("ga", "gb", "gc"))
    QY = Queue(pl, "y", ("ya", "yb", "yc"))

    def t_mm(out, lhsT, rhs, start, stop, tp=None, inc=False):
        if inc:
            val = pl.inc("pe")
            h = sems["pe"]
            pl.emit("t", lambda E, o=out, l=lhsT, r=rhs, a=start, z=stop, tp=tp,
                    h=h: E.matmul(o, l, r, start=a, stop=z, tile_position=tp
                                  ).then_inc(h, 1))
            return val
        pl.emit("t", lambda E, o=out, l=lhsT, r=rhs, a=start, z=stop, tp=tp:
                E.matmul(o, l, r, start=a, stop=z, tile_position=tp))
        return None

    def v_op(fn, inc=True):
        pl.drain("v")
        if inc:
            val = pl.inc("ve")
            h = sems["ve"]
            pl.emit("v", lambda E, fn=fn, h=h: fn(E).then_inc(h, 1))
            return val
        pl.emit("v", lambda E, fn=fn: fn(E))
        return None

    def s_op(fn, inc=True):
        pl.drain("s")
        if inc:
            val = pl.inc("se")
            h = sems["se"]
            pl.emit("s", lambda E, fn=fn, h=h: fn(E).then_inc(h, 1))
            return val
        pl.emit("s", lambda E, fn=fn: fn(E))
        return None

    tok_id = QY.batch([(ident_s, id_d)])
    tok_gwt = QG.batch([(gwt_s, gwt_d.rearrange("(c p) e -> p c e", p=128))])
    v_op(lambda E: E.memset(ones_s, 1.0), inc=False)
    v_op(lambda E: E.memset(consts, EPS), inc=False)
    v_op(lambda E: E.memset(combine, 0.0))

    def emit_ln(x_tok):
        pl.wait("t", "ve", pl.cnt["ve"])
        QY.wait_tok("t", x_tok)
        QY.wait_tok("s", x_tok)
        pe_after = None
        for k in range(NK):
            if pe_after:
                pl.wait("s", "pe", pe_after)
            se_sq = s_op(lambda E, k=k: E.activation(xsq, xres[:, k, :],
                                                     AF.Square))
            pl.wait("t", "se", se_sq)
            for J in range(4):
                t_mm(ps[0:1, J, :], ones_s, xres[:, k, J * 512:(J + 1) * 512],
                     start=(k == 0), stop=(k == NK - 1))
            for J in range(4):
                pe_after = t_mm(ps[0:1, 4 + J, :], ones_s,
                                xsq[:, J * 512:(J + 1) * 512],
                                start=(k == 0), stop=(k == NK - 1), inc=(J == 3))
        pl.wait("v", "pe", pe_after)
        for J in range(4):
            v_op(lambda E, J=J: E.tensor_copy(ln_m[:, J * 512:(J + 1) * 512],
                                              ps[0:1, J, :]), inc=False)
        for J in range(4):
            v_op(lambda E, J=J: E.tensor_copy(ln_v[:, J * 512:(J + 1) * 512],
                                              ps[0:1, 4 + J, :]),
                 inc=(J == 3))
        v_op(lambda E: E.tensor_scalar(ln_m[:], ln_m[:], 1.0 / 1024.0, None,
                                       ALU.mult), inc=False)
        v_op(lambda E: E.tensor_scalar(ln_v[:], ln_v[:], 1.0 / 1024.0, None,
                                       ALU.mult), inc=False)
        v_op(lambda E: E.tensor_tensor(xsq[0:1, :], ln_m[:], ln_m[:], ALU.mult),
             inc=False)
        ve_var = v_op(lambda E: E.tensor_tensor(ln_v[:], ln_v[:], xsq[0:1, :],
                                                ALU.subtract))
        pl.wait("s", "ve", ve_var)
        se_sd = s_op(lambda E: E.activation(ln_v[:], ln_v[:], AF.Sqrt,
                                            bias=consts[0:1, 0:1]))
        pl.wait("v", "se", se_sd)
        ve_rstd = v_op(lambda E: E.reciprocal(ln_v[:], ln_v[:]))
        tok_w = QY.batch([(scr_d[0:1, :], ln_m[:]), (scr_d[1:2, :], ln_v[:])],
                         gate=("ve", ve_rstd))
        ve_norm = None
        for h in range(2):
            sl = slice(h * JBW, (h + 1) * JBW)
            gate = ("ve", ve_norm) if ve_norm else None
            gates = [tok_w] + ([gate] if gate else [])
            tok_r = QY.batch([
                (ln_b[:, 0, :], bass.AP(scr_d.tensor, h * JBW,
                                        [[0, 128], [1, JBW]])),
                (ln_b[:, 1, :], bass.AP(scr_d.tensor, S + h * JBW,
                                        [[0, 128], [1, JBW]])),
            ], gate=gates)
            QY.wait_tok("v", tok_r)
            for k in range(NK):
                v_op(lambda E, k=k, sl=sl: E.tensor_tensor(
                    xsq[:, 0:JBW], xres[:, k, sl], ln_b[:, 0, :], ALU.subtract),
                    inc=False)
                ve_norm = v_op(lambda E, k=k, sl=sl: E.tensor_tensor(
                    xln[:, k, sl], xsq[:, 0:JBW], ln_b[:, 1, :], ALU.mult))
        return ve_norm, tok_w

    proj_bank_ve = {}
    obank_ve = [0]

    def emit_pair(p, uv_tok, ln_done):
        pl.wait("t", "ve", ln_done)
        QG.wait_tok("t", uv_tok)
        for (wsrc, dst) in ((uT_s, qzu_r), (vT_s, qzv_r)):
            for J in range(4):
                bk = 5 + (J % 2)
                pl.wait("t", "ve", proj_bank_ve.get(bk, 0))
                for k in range(NK):
                    pe_w = t_mm(bank(bk), wsrc[:, k, p * 128:(p + 1) * 128],
                                xln[:, k, J * 512:(J + 1) * 512],
                                start=(k == 0), stop=(k == NK - 1),
                                inc=(k == NK - 1))
                pl.wait("v", "pe", pe_w)
                proj_bank_ve[bk] = v_op(
                    lambda E, dst=dst, J=J, bk=bk: E.tensor_scalar(
                        dst[:, J * 512:(J + 1) * 512], bank(bk), SCALE, None,
                        ALU.mult))
        proj_done = max(v for v in proj_bank_ve.values())

        tok_bids = []
        for i in range(16):
            tok_bids.append(QY.batch(
                [(u_tok[:, i, :], qzu_r[:, i * 128:(i + 1) * 128]),
                 (v_tok[:, i, :], qzv_r[:, i * 128:(i + 1) * 128])],
                gate=("ve", proj_done), transpose=True))

        pl.wait("t", "ve", proj_done)
        for passi in range(2):
            lhs_r, rhs_r = (qzu_r, qzv_r) if passi == 0 else (qzv_r, qzu_r)
            tok = u_tok if passi == 0 else v_tok
            qidx = 1 if passi == 0 else 0
            g = 0
            for j in range(4):
                pl.wait("t", "ve", obank_ve[0])
                for i in range(16):
                    par = g % 2
                    t_mm(bank(2 * par), lhs_r[0:64, i * 128:(i + 1) * 128],
                         rhs_r[0:64, j * 512:(j + 1) * 512], True, True,
                         tp=(0, 0))
                    pe_f = t_mm(bank(2 * par + 1),
                                lhs_r[64:128, i * 128:(i + 1) * 128],
                                rhs_r[64:128, j * 512:(j + 1) * 512], True, True,
                                tp=(64, 0), inc=True)
                    pl.wait("s", "pe", pe_f)
                    se_s = s_op(lambda E, par=par: E.activation(
                        qh_st[:, par, :], bank(2 * par, 2), AF.Sigmoid))
                    pl.wait("t", "se", se_s)
                    QY.wait_tok("t", tok_bids[i])
                    t_mm(ps[0:64, 4, :], tok[:, i, 0:64], qh_st[:, par, 0:512],
                         start=(i == 0), stop=(i == 15), tp=(0, 0))
                    pe_o = t_mm(ps[64:128, 4, :], tok[:, i, 64:128],
                                qh_st[:, par, 512:1024], start=(i == 0),
                                stop=(i == 15), tp=(0, 64), inc=(i == 15))
                    g += 1
                pl.wait("v", "pe", pe_o)
                obank_ve[0] = v_op(lambda E, qidx=qidx, p=p, j=j: E.tensor_copy(
                    qhv[:, qidx, p, j * 512:(j + 1) * 512], bank(4)))

    def emit_final(x_tok, uvn_tok):
        pl.wait("t", "ve", obank_ve[0])
        QG.wait_tok("t", uvn_tok)
        prev = {}
        last_ve = None
        for k in range(NK):
            for J in range(4):
                bk = 5 + (J % 2)
                pl.wait("t", "ve", prev.get(bk, 0))
                for c in range(NK):
                    t_mm(bank(bk), u_s[:, c, k * 128:(k + 1) * 128],
                         qhv[:, 0, c, J * 512:(J + 1) * 512],
                         start=(c == 0), stop=False)
                    pe_w = t_mm(bank(bk), v_s[:, c, k * 128:(k + 1) * 128],
                                qhv[:, 1, c, J * 512:(J + 1) * 512],
                                start=False, stop=(c == NK - 1),
                                inc=(c == NK - 1))
                pl.wait("v", "pe", pe_w)
                QY.wait_tok("v", x_tok)
                last_ve = v_op(lambda E, k=k, J=J, bk=bk: E.scalar_tensor_tensor(
                    xres[:, k, J * 512:(J + 1) * 512], bank(bk), SCALE,
                    xres[:, k, J * 512:(J + 1) * 512], ALU.mult, ALU.add))
                prev[bk] = last_ve
        return last_ve

    def emit_router(b, ln_done):
        pl.wait("t", "ve", ln_done)
        QG.wait_tok("t", tok_gwt)
        prev_copy = {}
        pe_tr = None
        for i in range(16):
            bk = i % 2
            pl.wait("t", "ve", prev_copy.get(bk, 0))
            for k in range(NK):
                pe_l = t_mm(ps[:, bk, 0:8], xln[:, k, i * 128:(i + 1) * 128],
                            gwt_s[:, k, :], start=(k == 0), stop=(k == NK - 1),
                            inc=(k == NK - 1))
            pl.wait("v", "pe", pe_l)
            prev_copy[bk] = v_op(lambda E, bk=bk: E.tensor_copy(rt_l,
                                                                ps[:, bk, 0:8]))
            v_op(lambda E: E.reduce_max(rt_nm1, rt_l, axis=AX.X, negate=True),
                 inc=False)
            ve_m1 = v_op(lambda E: E.tensor_scalar(rt_t, rt_l, rt_nm1[:, 0:1],
                                                   None, ALU.add))
            pl.wait("s", "ve", ve_m1)
            s_op(lambda E: E.activation(rt_p, rt_l, AF.Exp,
                                        bias=rt_nm1[:, 0:1]), inc=False)
            v_op(lambda E: E.tensor_scalar(rt_t, rt_t, 0.0, -1e30, ALU.is_equal,
                                           ALU.mult), inc=False)
            v_op(lambda E: E.tensor_tensor(rt_t, rt_l, rt_t, ALU.add), inc=False)
            ve_m2 = v_op(lambda E: E.reduce_max(rt_nm2, rt_t, axis=AX.X,
                                                negate=True))
            pl.wait("s", "ve", ve_m2)
            s_op(lambda E: E.activation(rt_e2, rt_nm2, AF.Exp, scale=-1.0,
                                        bias=rt_nm1[:, 0:1]), inc=False)
            se_den = s_op(lambda E: E.add(rt_e2, rt_e2, 1.0))
            pl.wait("v", "se", se_den)
            v_op(lambda E: E.reciprocal(rt_w, rt_e2), inc=False)
            v_op(lambda E: E.tensor_scalar(rt_t, rt_l, rt_nm2[:, 0:1], None,
                                           ALU.add), inc=False)
            v_op(lambda E: E.tensor_scalar(rt_t, rt_t, 0.0, None, ALU.is_ge),
                 inc=False)
            v_op(lambda E: E.tensor_tensor(rt_p, rt_p, rt_t, ALU.mult),
                 inc=False)
            v_op(lambda E: E.tensor_scalar(rt_w, rt_w, 2.5, None, ALU.mult),
                 inc=False)
            ve_c = v_op(lambda E, i=i: E.tensor_scalar(
                combine[:, i, :], rt_p, rt_w[:, 0:1], None, ALU.mult))
            pl.wait("t", "ve", ve_c)
            pe_tr = pl.inc("pe")
            h = sems["pe"]
            pl.emit("t", lambda E, i=i, h=h: E.transpose(
                ps[0:8, 4 + i // 4, (i % 4) * 128:(i % 4 + 1) * 128],
                combine[:, i, :], ident_s).then_inc(h, 1))
        pl.wait("v", "pe", pe_tr)
        ve_cb = v_op(lambda E: E.tensor_copy(
            cb_f, ps[0:8, 4:8, :].rearrange("p a b -> p (a b)")))
        tok_cb = QY.batch([(scrcb_d, cb_f)], gate=("ve", ve_cb))
        if DEBUG:
            QY.batch([(dbg["cb"][b], cb_f)])
        return tok_cb, ve_cb

    down_done = [0]

    def emit_ffn(b, tok_cb, ve_cb):
        out_toks = []
        down_done[0] = max(down_done[0], pl.cnt["pe"])
        for jb in range(2):
            sl = slice(jb * JBW, (jb + 1) * JBW)
            pl.wait("t", "ve", pl.cnt["ve"])

            def wsrc(e):
                if e < 8:
                    return wg_d[e], wu_d[e], wd_d[e]
                h = e - 8
                return (swg_d[:, h * 1408:(h + 1) * 1408],
                        swu_d[:, h * 1408:(h + 1) * 1408],
                        swd_d[h * 1408:(h + 1) * 1408, :])

            def load_wchunk(e, f, gate):
                wgm, wum, _ = wsrc(e)
                par = f % 2
                return QG.batch([
                    (wgc[:, par], wgm[:, f * 128:(f + 1) * 128].rearrange(
                        "(c p) f -> p c f", p=128)),
                    (wuc[:, par], wum[:, f * 128:(f + 1) * 128].rearrange(
                        "(c p) f -> p c f", p=128)),
                ], gate=gate)

            def load_wdchunk(e, f, gate):
                _, _, wdm = wsrc(e)
                return QG.batch([(wdc[:, f, :],
                                  wdm[f * 128:(f + 1) * 128, :])], gate=gate)

            def load_cb(e, gate):
                return QY.batch([(cb_b[:, e % 2, :], bass.AP(
                    scrcb_d.tensor, e * S + jb * JBW, [[0, 128], [1, JBW]]))],
                    gate=gate)

            bid_w = {(0, 0): load_wchunk(0, 0, ("ve", pl.cnt["ve"])),
                     (0, 1): load_wchunk(0, 1, ("ve", pl.cnt["ve"]))}
            bid_cb = {0: load_cb(0, [("ve", pl.cnt["ve"]), tok_cb])}
            bid_wd = {}
            prev_sig = {}
            cb_used = {}
            ve_ttB = [0]
            ve_yacc = {}
            for e in range(NE):
                for f in range(NF):
                    par = f % 2
                    bid_wd[(e, f)] = load_wdchunk(e, f, ("pe", down_done[0]))
                    if e + 1 < 8 and f == 0 and (e + 1) not in bid_cb:
                        bid_cb[e + 1] = load_cb(
                            e + 1, ("ve", cb_used.get((e + 1) % 2, 0)))
                    QG.wait_tok("t", bid_w[(e, f)])
                    pl.wait("t", "se", prev_sig.get("A", 0))
                    for h2 in range(2):
                        for k in range(NK):
                            pe_g = t_mm(bank(h2), wgc[:, par, k, :],
                                        xln[:, k, jb * JBW + h2 * 512:
                                            jb * JBW + (h2 + 1) * 512],
                                        start=(k == 0), stop=(k == NK - 1),
                                        inc=(k == NK - 1 and h2 == 1))
                    pl.wait("t", "ve", ve_ttB[0])
                    for h2 in range(2):
                        for k in range(NK):
                            pe_u = t_mm(bank(2 + h2), wuc[:, par, k, :],
                                        xln[:, k, jb * JBW + h2 * 512:
                                            jb * JBW + (h2 + 1) * 512],
                                        start=(k == 0), stop=(k == NK - 1),
                                        inc=(k == NK - 1 and h2 == 1))
                    pl.wait("s", "pe", pe_g)
                    pl.wait("s", "ve", ve_ttB[0])
                    se_si = s_op(lambda E: E.activation(tmpS, bank(0, 2),
                                                        AF.Silu))
                    prev_sig["A"] = se_si
                    prev_sig[par] = se_si
                    pl.wait("v", "pe", pe_u)
                    if e < 8:
                        QY.wait_tok("v", bid_cb[e])
                        v_op(lambda E, e=e: E.tensor_tensor(
                            tmp2, bank(2, 2), cb_b[:, e % 2, :], ALU.mult),
                            inc=False)
                        pl.wait("v", "se", se_si)
                        ve_g = v_op(lambda E, f=f: E.tensor_tensor(
                            g_st[:, f, :], tmpS, tmp2, ALU.mult))
                        cb_used[e % 2] = ve_g
                    else:
                        pl.wait("v", "se", se_si)
                        ve_g = v_op(lambda E, f=f: E.tensor_tensor(
                            g_st[:, f, :], tmpS, bank(2, 2), ALU.mult))
                    ve_ttB[0] = ve_g
                    if f + 2 < NF:
                        if (e, f + 2) not in bid_w:
                            bid_w[(e, f + 2)] = load_wchunk(e, f + 2,
                                                           ("ve", ve_g))
                    elif f == NF - 1 and e + 1 < NE:
                        for f2 in (0, 1):
                            bid_w[(e + 1, f2)] = load_wchunk(e + 1, f2,
                                                             ("ve", ve_g))
                pl.wait("t", "ve", ve_g)
                for k in range(NK):
                    db = 4 + 2 * (k % 2)
                    pl.wait("t", "ve", ve_yacc.get(db, 0))
                    for ff in (NF - 3, NF - 2, NF - 1):
                        QG.wait_tok("t", bid_wd[(e, ff)])
                    for f in range(NF):
                        for h2 in range(2):
                            pe_d = t_mm(bank(db + h2),
                                        wdc[:, f, k * 128:(k + 1) * 128],
                                        g_st[:, f, h2 * 512:(h2 + 1) * 512],
                                        start=(f == 0), stop=(f == NF - 1),
                                        inc=(f == NF - 1 and h2 == 1))
                    pl.wait("v", "pe", pe_d)
                    if e == 0:
                        ve_y = v_op(lambda E, k=k, db=db: E.tensor_copy(
                            yacc[:, k, :], bank(db, 2)))
                    else:
                        ve_y = v_op(lambda E, k=k, db=db: E.tensor_tensor(
                            yacc[:, k, :], yacc[:, k, :], bank(db, 2), ALU.add))
                    ve_yacc[db] = ve_y
                    down_done[0] = pe_d
            ve_f = None
            for k in range(NK):
                ve_f = v_op(lambda E, k=k, sl=sl: E.tensor_tensor(
                    xres[:, k, sl], xres[:, k, sl], yacc[:, k, :], ALU.add))
            out_toks.append(QY.batch(
                [(out_d[b].rearrange("(c p) s -> p c s", p=128)[:, :, sl],
                  xres[:, :, sl])], gate=("ve", ve_f)))
            if DEBUG:
                QY.batch([(dbg["x2"][b].rearrange("c p s -> p c s")[:, :, sl],
                           xres[:, :, sl])], gate=("ve", ve_f))
        return out_toks

    prev_out = []
    for b in range(NB):
        x_tok = QY.batch([(xres[:],
                           x_d[b].rearrange("(c p) s -> p c s", p=128))],
                         gate=prev_out if prev_out else None)
        ln_done, ln_tok_w = emit_ln(x_tok)
        if DEBUG:
            for k in range(NK):
                QY.batch([(dbg["xln"][b, 0, k], xln[:, k, :])],
                         gate=("ve", ln_done))
        uv_tok = QG.batch([(uT_s, ut_d.rearrange("(c p) q -> p c q", p=128)),
                           (vT_s, vt_d.rearrange("(c p) q -> p c q", p=128))],
                          gate=[("ve", ln_done), ln_tok_w])
        for p in range(NPAIR):
            emit_pair(p, uv_tok, ln_done)
            if DEBUG and p == 0:
                QY.batch([(dbg["qz"][b, 0], qzu_r[:]),
                          (dbg["qz"][b, 1], qzv_r[:])],
                         gate=("ve", max(proj_bank_ve.values())))
        last_pe = pl.cnt["pe"]
        uvn_tok = QG.batch([(u_s, u_d.rearrange("(c p) q -> p c q", p=128)),
                            (v_s, v_d.rearrange("(c p) q -> p c q", p=128))],
                           gate=("pe", last_pe))
        x2_tok = QY.batch([(xres[:],
                            x_d[b].rearrange("(c p) s -> p c s", p=128))],
                          gate=("pe", last_pe))
        if DEBUG:
            for qi in range(2):
                for c in range(NK):
                    QY.batch([(dbg["qhv"][b, qi, c], qhv[:, qi, c, :])],
                             gate=("ve", obank_ve[0]))
        fin_ve = emit_final(x2_tok, uvn_tok)
        pl.wait("t", "ve", fin_ve)
        pl.wait("s", "ve", fin_ve)
        ln2_done, _ = emit_ln(None)
        if DEBUG:
            for k in range(NK):
                QY.batch([(dbg["xln"][b, 1, k], xln[:, k, :])],
                         gate=("ve", ln2_done))
        tok_cb, ve_cb = emit_router(b, ln2_done)
        prev_out = emit_ffn(b, tok_cb, ve_cb)

    QG.flush()
    QY.flush()
    finh = sems["fin"]
    for eng in ENGS:
        pl.emit(eng, lambda E, h=finh: E.sem_inc(h, 1))
    pl.emit("g", lambda E, h=finh: E.wait_ge(h, 5))

    with ctx:
        with nc.Block() as block:
            @block.tensor
            def _(t):
                for fn in pl.q["t"]:
                    fn(t)

            @block.vector
            def _(v):
                for fn in pl.q["v"]:
                    fn(v)

            @block.scalar
            def _(s):
                for fn in pl.q["s"]:
                    fn(s)

            @block.sync
            def _(y):
                for fn in pl.q["y"]:
                    fn(y)

            @block.gpsimd
            def _(g):
                for fn in pl.q["g"]:
                    fn(g)
    return nc


_CACHE = {}


def kernel(L0_attn_qz, L0_attn_ternary_factor_u, L0_attn_ternary_factor_v,
           L0_ffn_gate_w, L0_ffn_w_gate, L0_ffn_w_up, L0_ffn_w_down,
           L0_ffn_sw_gate, L0_ffn_sw_up, L0_ffn_sw_down):
    x = np.ascontiguousarray(np.transpose(np.asarray(L0_attn_qz, np.float32),
                                          (0, 2, 1)))
    u = np.ascontiguousarray(np.asarray(L0_attn_ternary_factor_u, np.float32))
    v = np.ascontiguousarray(np.asarray(L0_attn_ternary_factor_v, np.float32))
    common = {
        "u": u, "v": v,
        "ut": np.ascontiguousarray(u.T),
        "vt": np.ascontiguousarray(v.T),
        "gwt": np.ascontiguousarray(np.asarray(L0_ffn_gate_w, np.float32).T),
        "ident": np.eye(128, dtype=np.float32),
        "wg": np.ascontiguousarray(np.asarray(L0_ffn_w_gate, np.float32)),
        "wu": np.ascontiguousarray(np.asarray(L0_ffn_w_up, np.float32)),
        "wd": np.ascontiguousarray(np.asarray(L0_ffn_w_down, np.float32)),
        "swg": np.ascontiguousarray(np.asarray(L0_ffn_sw_gate, np.float32)),
        "swu": np.ascontiguousarray(np.asarray(L0_ffn_sw_up, np.float32)),
        "swd": np.ascontiguousarray(np.asarray(L0_ffn_sw_down, np.float32)),
    }
    if "nc" not in _CACHE:
        _CACHE["nc"] = build_program()
    nc = _CACHE["nc"]
    in_maps = []
    for c in range(8):
        m = dict(common)
        m["x"] = np.ascontiguousarray(x[2 * c:2 * c + 2])
        in_maps.append(m)
    res = run_bass_kernel_spmd(nc, in_maps, core_ids=list(range(8)))
    out = np.empty((16, 1024, 2048), np.float32)
    for c in range(8):
        out[2 * c:2 * c + 2] = res.results[c]["out"]
    return np.ascontiguousarray(np.transpose(out, (0, 2, 1)))


# version 03
# speedup vs baseline: 1.2073x; 1.2073x over previous
import numpy as np
from contextlib import ExitStack

import concourse.bass as bass
import concourse.mybir as mybir
from concourse.bass_utils import run_bass_kernel_spmd

F32 = mybir.dt.float32
BF16 = mybir.dt.bfloat16
AF = mybir.ActivationFunctionType
ALU = mybir.AluOpType
AX = mybir.AxisListType

NB = 2
S = 2048
NK = 8
NPAIR = 8
NF = 11
NE = 10
SCALE = 0.03125
EPS = 1e-5
JBW = 1024
DEBUG = False

ENGS = ("t", "v", "s", "y", "g")


class Plan:
    def __init__(self, nc, sems):
        self.nc = nc
        self.sems = sems
        self.cnt = {k: 0 for k in sems}
        self.q = {e: [] for e in ENGS}
        self.lastwait = {}

    def emit(self, eng, fn):
        self.q[eng].append(fn)
        self.drained = getattr(self, "drained", {})
        self.drained[eng] = False

    def drain(self, eng):
        d = getattr(self, "drained", {})
        if d.get(eng):
            return
        self.q[eng].append(lambda E: E.drain())
        d[eng] = True
        self.drained = d

    def wait(self, eng, sem, val):
        if val is None or val <= 0:
            return
        key = (eng, sem)
        if self.lastwait.get(key, -1) >= val:
            return
        self.lastwait[key] = val
        h = self.sems[sem]
        self.emit(eng, lambda E, h=h, v=val: E.wait_ge(h, v))

    def inc(self, sem, n=1):
        self.cnt[sem] += n
        return self.cnt[sem]


class Queue:
    R = 3

    def __init__(self, plan, eng, semnames):
        assert len(semnames) == self.R
        self.p = plan
        self.eng = eng
        self.names = semnames
        self.iss = [0] * self.R
        self.n = 0

    def batch(self, dmas, gate=None, transpose=False):
        pl = self.p
        r = self.n % self.R
        sem = self.names[r]
        pl.wait(self.eng, sem, 16 * self.iss[r])
        if gate is not None:
            gates = gate if isinstance(gate, list) else [gate]
            for gsem, gval in gates:
                pl.wait(self.eng, gsem, gval)
        h = pl.sems[sem]
        for (o, i) in dmas:
            pl.inc(sem, 16)
            if transpose:
                pl.emit(self.eng, lambda E, o=o, i=i, h=h:
                        E.dma_start_transpose(o, i).then_inc(h, 16))
            else:
                pl.emit(self.eng, lambda E, o=o, i=i, h=h:
                        E.dma_start(out=o, in_=i).then_inc(h, 16))
            self.iss[r] += 1
        self.n += 1
        return (sem, 16 * self.iss[r])

    def wait_tok(self, eng, tok):
        if tok is not None:
            self.p.wait(eng, tok[0], tok[1])

    def flush(self):
        for r in range(self.R):
            self.p.wait(self.eng, self.names[r], 16 * self.iss[r])


def build_program():
    nc = bass.Bass("TRN2", target_bir_lowering=False)

    x_d = nc.dram_tensor("x", [NB, 1024, S], F32, kind="ExternalInput").ap()
    u_d = nc.dram_tensor("u", [1024, 1024], F32, kind="ExternalInput").ap()
    v_d = nc.dram_tensor("v", [1024, 1024], F32, kind="ExternalInput").ap()
    ut_d = nc.dram_tensor("ut", [1024, 1024], F32, kind="ExternalInput").ap()
    vt_d = nc.dram_tensor("vt", [1024, 1024], F32, kind="ExternalInput").ap()
    gwt_d = nc.dram_tensor("gwt", [1024, 8], F32, kind="ExternalInput").ap()
    wg_d = nc.dram_tensor("wg", [8, 1024, 1408], F32, kind="ExternalInput").ap()
    wu_d = nc.dram_tensor("wu", [8, 1024, 1408], F32, kind="ExternalInput").ap()
    wd_d = nc.dram_tensor("wd", [8, 1408, 1024], F32, kind="ExternalInput").ap()
    swg_d = nc.dram_tensor("swg", [1024, 2816], F32, kind="ExternalInput").ap()
    swu_d = nc.dram_tensor("swu", [1024, 2816], F32, kind="ExternalInput").ap()
    swd_d = nc.dram_tensor("swd", [2816, 1024], F32, kind="ExternalInput").ap()
    id_d = nc.dram_tensor("ident", [128, 128], F32, kind="ExternalInput").ap()
    out_d = nc.dram_tensor("out", [NB, 1024, S], F32, kind="ExternalOutput").ap()
    scr_d = nc.dram_tensor("scr", [2, S], F32, kind="Internal").ap()
    scrcb_d = nc.dram_tensor("scrcb", [8, S], F32, kind="Internal").ap()

    dbg = {}
    if DEBUG:
        dbg["xln"] = nc.dram_tensor("d_xln", [NB, 2, NK, 128, S], F32,
                                    kind="ExternalOutput").ap()
        dbg["qz"] = nc.dram_tensor("d_qz", [NB, 2, 128, S], F32,
                                   kind="ExternalOutput").ap()
        dbg["qhv"] = nc.dram_tensor("d_qhv", [NB, 2, NK, 128, S], F32,
                                    kind="ExternalOutput").ap()
        dbg["cb"] = nc.dram_tensor("d_cb", [NB, 8, S], F32,
                                   kind="ExternalOutput").ap()
        dbg["x2"] = nc.dram_tensor("d_x2", [NB, NK, 128, S], F32,
                                   kind="ExternalOutput").ap()

    BASE, TOP = 22944, 229344
    cur = [BASE]

    def sb(name, shape, dtype, at=None):
        nbytes = int(np.prod(shape[1:])) * (2 if dtype == BF16 else 4)
        if at is None:
            at = cur[0]
            cur[0] += (nbytes + 31) & ~31
        assert at + nbytes <= TOP, (name, at, nbytes)
        return nc.alloc_sbuf_tensor_at(name, shape, dtype, offset=at).ap()

    ident_s = sb("ident_s", [128, 128], F32)
    gwt_s = sb("gwt_s", [128, 8, 8], BF16)
    ones_s = sb("ones_s", [128, 1], F32)
    consts = sb("consts", [1, 8], F32)
    xln = sb("xln", [128, NK, S], BF16)
    tmpS = sb("tmpS", [128, JBW], F32)
    tmp2 = sb("tmp2", [128, JBW], F32)
    rt_l = sb("rt_l", [128, 8], F32)
    rt_nm1 = sb("rt_nm1", [128, 1], F32)
    rt_nm2 = sb("rt_nm2", [128, 1], F32)
    rt_p = sb("rt_p", [128, 8], F32)
    rt_t = sb("rt_t", [128, 8], F32)
    rt_e2 = sb("rt_e2", [128, 1], F32)
    rt_w = sb("rt_w", [128, 1], F32)
    combine = sb("combine", [128, 16, 8], F32)

    R0 = cur[0]; cur[0] += 65536
    xres = sb("xres", [128, NK, S], F32, at=R0)
    qzu_r = sb("qzu_r", [128, S], BF16, at=R0)
    qzv_r = sb("qzv_r", [128, S], BF16, at=R0 + 4096)
    u_tok = sb("u_tok", [128, 16, 128], BF16, at=R0 + 8192)
    v_tok = sb("v_tok", [128, 16, 128], BF16, at=R0 + 12288)
    qh_st = sb("qh_st", [128, 2, 1024], BF16, at=R0 + 16384)

    R1 = cur[0]; cur[0] += 65536
    qhv = sb("qhv", [128, 2, NK, S], BF16, at=R1)
    g_st = sb("g_st", [128, NF, JBW], BF16, at=R1)
    wgc = sb("wgc", [128, 2, 8, 128], BF16, at=R1 + 22528)
    wuc = sb("wuc", [128, 2, 8, 128], BF16, at=R1 + 26624)
    yacc = sb("yacc", [128, NK, JBW], F32, at=R1 + 30720)

    R2 = cur[0]; cur[0] += 32768
    ln_m = sb("ln_m", [1, S], F32, at=R2)
    ln_v = sb("ln_v", [1, S], F32, at=R2 + 8192)
    xsq = sb("xsq", [128, S], F32, at=R2 + 16384)
    ln_b = sb("ln_b", [128, 2, JBW], F32, at=R2 + 24576)
    uT_s = sb("uT_s", [128, NK, 1024], BF16, at=R2)
    vT_s = sb("vT_s", [128, NK, 1024], BF16, at=R2 + 16384)
    u_s = sb("u_s", [128, NK, 1024], BF16, at=R2)
    v_s = sb("v_s", [128, NK, 1024], BF16, at=R2 + 16384)
    wdc = sb("wdc", [128, NF, 1024], BF16, at=R2)
    cb_b = sb("cb_b", [128, 2, JBW], F32, at=R2 + 22528)
    cb_f = sb("cb_f", [8, S], F32, at=R2 + 22528)
    assert cur[0] <= TOP, cur[0]

    ps = nc.alloc_psum_tensor("ps", [128, 8, 512], F32).ap()

    def bank(k, w=1):
        if w == 1:
            return ps[:, k, :]
        return ps[:, k:k + w, :].rearrange("p a b -> p (a b)")

    ctx = ExitStack()
    sems = {}
    for name in ("ga", "gb", "gc", "ya", "yb", "yc", "pe", "se", "ve", "fin"):
        sems[name] = ctx.enter_context(nc.semaphore(name))
    pl = Plan(nc, sems)
    QG = Queue(pl, "g", ("ga", "gb", "gc"))
    QY = Queue(pl, "y", ("ya", "yb", "yc"))

    def t_mm(out, lhsT, rhs, start, stop, tp=None, inc=False):
        if inc:
            val = pl.inc("pe")
            h = sems["pe"]
            pl.emit("t", lambda E, o=out, l=lhsT, r=rhs, a=start, z=stop, tp=tp,
                    h=h: E.matmul(o, l, r, start=a, stop=z, tile_position=tp
                                  ).then_inc(h, 1))
            return val
        pl.emit("t", lambda E, o=out, l=lhsT, r=rhs, a=start, z=stop, tp=tp:
                E.matmul(o, l, r, start=a, stop=z, tile_position=tp))
        return None

    def v_op(fn, inc=True, dr=False):
        if dr:
            pl.drain("v")
        if inc:
            val = pl.inc("ve")
            h = sems["ve"]
            pl.emit("v", lambda E, fn=fn, h=h: fn(E).then_inc(h, 1))
            return val
        pl.emit("v", lambda E, fn=fn: fn(E))
        return None

    def s_op(fn, inc=True, dr=False):
        if dr:
            pl.drain("s")
        if inc:
            val = pl.inc("se")
            h = sems["se"]
            pl.emit("s", lambda E, fn=fn, h=h: fn(E).then_inc(h, 1))
            return val
        pl.emit("s", lambda E, fn=fn: fn(E))
        return None

    tok_id = QY.batch([(ident_s, id_d)])
    tok_gwt = QG.batch([(gwt_s, gwt_d.rearrange("(c p) e -> p c e", p=128))])
    v_op(lambda E: E.memset(ones_s, 1.0), inc=False)
    v_op(lambda E: E.memset(consts, EPS), inc=False)
    v_op(lambda E: E.memset(combine, 0.0))

    def emit_ln(x_tok):
        pl.wait("t", "ve", pl.cnt["ve"])
        QY.wait_tok("t", x_tok)
        QY.wait_tok("s", x_tok)
        pe_after = None
        for k in range(NK):
            if pe_after:
                pl.wait("s", "pe", pe_after)
            se_sq = s_op(lambda E, k=k: E.activation(xsq, xres[:, k, :],
                                                     AF.Square))
            pl.wait("t", "se", se_sq)
            for J in range(4):
                t_mm(ps[0:1, J, :], ones_s, xres[:, k, J * 512:(J + 1) * 512],
                     start=(k == 0), stop=(k == NK - 1))
            for J in range(4):
                pe_after = t_mm(ps[0:1, 4 + J, :], ones_s,
                                xsq[:, J * 512:(J + 1) * 512],
                                start=(k == 0), stop=(k == NK - 1), inc=(J == 3))
        pl.wait("v", "pe", pe_after)
        for J in range(4):
            v_op(lambda E, J=J: E.tensor_copy(ln_m[:, J * 512:(J + 1) * 512],
                                              ps[0:1, J, :]), inc=False)
        for J in range(4):
            v_op(lambda E, J=J: E.tensor_copy(ln_v[:, J * 512:(J + 1) * 512],
                                              ps[0:1, 4 + J, :]),
                 inc=(J == 3))
        v_op(lambda E: E.tensor_scalar(ln_m[:], ln_m[:], 1.0 / 1024.0, None,
                                       ALU.mult), inc=False, dr=True)
        v_op(lambda E: E.tensor_scalar(ln_v[:], ln_v[:], 1.0 / 1024.0, None,
                                       ALU.mult), inc=False, dr=True)
        v_op(lambda E: E.tensor_tensor(xsq[0:1, :], ln_m[:], ln_m[:], ALU.mult),
             inc=False, dr=True)
        ve_var = v_op(lambda E: E.tensor_tensor(ln_v[:], ln_v[:], xsq[0:1, :],
                                                ALU.subtract), dr=True)
        pl.wait("s", "ve", ve_var)
        se_sd = s_op(lambda E: E.activation(ln_v[:], ln_v[:], AF.Sqrt,
                                            bias=consts[0:1, 0:1]))
        pl.wait("v", "se", se_sd)
        ve_rstd = v_op(lambda E: E.reciprocal(ln_v[:], ln_v[:]))
        tok_w = QY.batch([(scr_d[0:1, :], ln_m[:]), (scr_d[1:2, :], ln_v[:])],
                         gate=("ve", ve_rstd))
        ve_norm = None
        for h in range(2):
            sl = slice(h * JBW, (h + 1) * JBW)
            gate = ("ve", ve_norm) if ve_norm else None
            gates = [tok_w] + ([gate] if gate else [])
            tok_r = QY.batch([
                (ln_b[:, 0, :], bass.AP(scr_d.tensor, h * JBW,
                                        [[0, 128], [1, JBW]])),
                (ln_b[:, 1, :], bass.AP(scr_d.tensor, S + h * JBW,
                                        [[0, 128], [1, JBW]])),
            ], gate=gates)
            QY.wait_tok("v", tok_r)
            for k in range(NK):
                v_op(lambda E, k=k, sl=sl: E.tensor_tensor(
                    xsq[:, 0:JBW], xres[:, k, sl], ln_b[:, 0, :], ALU.subtract),
                    inc=False, dr=True)
                ve_norm = v_op(lambda E, k=k, sl=sl: E.tensor_tensor(
                    xln[:, k, sl], xsq[:, 0:JBW], ln_b[:, 1, :], ALU.mult),
                    dr=True)
        return ve_norm, tok_w

    proj_bank_ve = {}
    obank_ve = [0]

    def emit_pair(p, uv_tok, ln_done):
        pl.wait("t", "ve", ln_done)
        QG.wait_tok("t", uv_tok)
        for (wsrc, dst) in ((uT_s, qzu_r), (vT_s, qzv_r)):
            for Jp in range(2):
                J0, J1 = 2 * Jp, 2 * Jp + 1
                pl.wait("t", "ve", proj_bank_ve.get(5, 0))
                pl.wait("t", "ve", proj_bank_ve.get(6, 0))
                for k in range(NK):
                    t_mm(bank(5), wsrc[:, k, p * 128:(p + 1) * 128],
                         xln[:, k, J0 * 512:(J0 + 1) * 512],
                         start=(k == 0), stop=(k == NK - 1))
                    pe_w = t_mm(bank(6), wsrc[:, k, p * 128:(p + 1) * 128],
                                xln[:, k, J1 * 512:(J1 + 1) * 512],
                                start=(k == 0), stop=(k == NK - 1),
                                inc=(k == NK - 1))
                pl.wait("v", "pe", pe_w)
                proj_bank_ve[5] = v_op(
                    lambda E, dst=dst, J=J0: E.tensor_scalar(
                        dst[:, J * 512:(J + 1) * 512], bank(5), SCALE, None,
                        ALU.mult))
                proj_bank_ve[6] = v_op(
                    lambda E, dst=dst, J=J1: E.tensor_scalar(
                        dst[:, J * 512:(J + 1) * 512], bank(6), SCALE, None,
                        ALU.mult))
        proj_done = max(v for v in proj_bank_ve.values())

        tok_bids = []
        for i in range(16):
            tok_bids.append(QY.batch(
                [(u_tok[:, i, :], qzu_r[:, i * 128:(i + 1) * 128]),
                 (v_tok[:, i, :], qzv_r[:, i * 128:(i + 1) * 128])],
                gate=("ve", proj_done), transpose=True))

        pl.wait("t", "ve", proj_done)
        for passi in range(2):
            lhs_r, rhs_r = (qzu_r, qzv_r) if passi == 0 else (qzv_r, qzu_r)
            tok = u_tok if passi == 0 else v_tok
            qidx = 1 if passi == 0 else 0
            slots = [(j, i) for j in range(4) for i in range(16)]
            se_sig = []

            def emit_F(g):
                j, i = slots[g]
                par = g % 2
                t_mm(bank(2 * par), lhs_r[0:64, i * 128:(i + 1) * 128],
                     rhs_r[0:64, j * 512:(j + 1) * 512], True, True, tp=(0, 0))
                return t_mm(bank(2 * par + 1),
                            lhs_r[64:128, i * 128:(i + 1) * 128],
                            rhs_r[64:128, j * 512:(j + 1) * 512], True, True,
                            tp=(64, 0), inc=True)

            pe_fs = [emit_F(0)]
            for g, (j, i) in enumerate(slots):
                par = g % 2
                if i == 0:
                    pl.wait("t", "ve", obank_ve[0])
                if g + 1 < len(slots):
                    pe_fs.append(emit_F(g + 1))
                pl.wait("s", "pe", pe_fs[g])
                se_s = s_op(lambda E, par=par: E.activation(
                    qh_st[:, par, :], bank(2 * par, 2), AF.Sigmoid))
                se_sig.append(se_s)
                pl.wait("t", "se", se_s)
                QY.wait_tok("t", tok_bids[i])
                t_mm(ps[0:64, 4, :], tok[:, i, 0:64], qh_st[:, par, 0:512],
                     start=(i == 0), stop=(i == 15), tp=(0, 0))
                pe_o = t_mm(ps[64:128, 4, :], tok[:, i, 64:128],
                            qh_st[:, par, 512:1024], start=(i == 0),
                            stop=(i == 15), tp=(0, 64), inc=(i == 15))
                if i == 15:
                    pl.wait("v", "pe", pe_o)
                    obank_ve[0] = v_op(
                        lambda E, qidx=qidx, p=p, j=j: E.tensor_copy(
                            qhv[:, qidx, p, j * 512:(j + 1) * 512], bank(4)))

    def emit_final(x_tok, uvn_tok):
        pl.wait("t", "ve", obank_ve[0])
        QG.wait_tok("t", uvn_tok)
        prev = {}
        last_ve = None
        for k in range(NK):
            for J in range(4):
                bk = 5 + (J % 2)
                pl.wait("t", "ve", prev.get(bk, 0))
                for c in range(NK):
                    t_mm(bank(bk), u_s[:, c, k * 128:(k + 1) * 128],
                         qhv[:, 0, c, J * 512:(J + 1) * 512],
                         start=(c == 0), stop=False)
                    pe_w = t_mm(bank(bk), v_s[:, c, k * 128:(k + 1) * 128],
                                qhv[:, 1, c, J * 512:(J + 1) * 512],
                                start=False, stop=(c == NK - 1),
                                inc=(c == NK - 1))
                pl.wait("v", "pe", pe_w)
                QY.wait_tok("v", x_tok)
                last_ve = v_op(lambda E, k=k, J=J, bk=bk: E.scalar_tensor_tensor(
                    xres[:, k, J * 512:(J + 1) * 512], bank(bk), SCALE,
                    xres[:, k, J * 512:(J + 1) * 512], ALU.mult, ALU.add))
                prev[bk] = last_ve
        return last_ve

    def emit_router(b, ln_done):
        pl.wait("t", "ve", ln_done)
        QG.wait_tok("t", tok_gwt)
        prev_copy = {}
        pe_tr = None
        for i in range(16):
            bk = i % 2
            pl.wait("t", "ve", prev_copy.get(bk, 0))
            for k in range(NK):
                pe_l = t_mm(ps[:, bk, 0:8], xln[:, k, i * 128:(i + 1) * 128],
                            gwt_s[:, k, :], start=(k == 0), stop=(k == NK - 1),
                            inc=(k == NK - 1))
            pl.wait("v", "pe", pe_l)
            prev_copy[bk] = v_op(lambda E, bk=bk: E.tensor_copy(rt_l,
                                                                ps[:, bk, 0:8]))
            v_op(lambda E: E.reduce_max(rt_nm1, rt_l, axis=AX.X, negate=True),
                 inc=False, dr=True)
            ve_m1 = v_op(lambda E: E.tensor_scalar(rt_t, rt_l, rt_nm1[:, 0:1],
                                                   None, ALU.add), dr=True)
            pl.wait("s", "ve", ve_m1)
            s_op(lambda E: E.activation(rt_p, rt_l, AF.Exp,
                                        bias=rt_nm1[:, 0:1]), inc=False)
            v_op(lambda E: E.tensor_scalar(rt_t, rt_t, 0.0, -1e30, ALU.is_equal,
                                           ALU.mult), inc=False, dr=True)
            v_op(lambda E: E.tensor_tensor(rt_t, rt_l, rt_t, ALU.add),
                 inc=False, dr=True)
            ve_m2 = v_op(lambda E: E.reduce_max(rt_nm2, rt_t, axis=AX.X,
                                                negate=True), dr=True)
            pl.wait("s", "ve", ve_m2)
            s_op(lambda E: E.activation(rt_e2, rt_nm2, AF.Exp, scale=-1.0,
                                        bias=rt_nm1[:, 0:1]), inc=False)
            se_den = s_op(lambda E: E.add(rt_e2, rt_e2, 1.0), dr=True)
            pl.wait("v", "se", se_den)
            v_op(lambda E: E.reciprocal(rt_w, rt_e2), inc=False, dr=True)
            v_op(lambda E: E.tensor_scalar(rt_t, rt_l, rt_nm2[:, 0:1], None,
                                           ALU.add), inc=False, dr=True)
            v_op(lambda E: E.tensor_scalar(rt_t, rt_t, 0.0, None, ALU.is_ge),
                 inc=False, dr=True)
            v_op(lambda E: E.tensor_tensor(rt_p, rt_p, rt_t, ALU.mult),
                 inc=False, dr=True)
            v_op(lambda E: E.tensor_scalar(rt_w, rt_w, 2.5, None, ALU.mult),
                 inc=False, dr=True)
            ve_c = v_op(lambda E, i=i: E.tensor_scalar(
                combine[:, i, :], rt_p, rt_w[:, 0:1], None, ALU.mult), dr=True)
            pl.wait("t", "ve", ve_c)
            pe_tr = pl.inc("pe")
            h = sems["pe"]
            pl.emit("t", lambda E, i=i, h=h: E.transpose(
                ps[0:8, 4 + i // 4, (i % 4) * 128:(i % 4 + 1) * 128],
                combine[:, i, :], ident_s).then_inc(h, 1))
        pl.wait("v", "pe", pe_tr)
        ve_cb = v_op(lambda E: E.tensor_copy(
            cb_f, ps[0:8, 4:8, :].rearrange("p a b -> p (a b)")))
        tok_cb = QY.batch([(scrcb_d, cb_f)], gate=("ve", ve_cb))
        if DEBUG:
            QY.batch([(dbg["cb"][b], cb_f)])
        return tok_cb, ve_cb

    down_done = [0]

    def emit_ffn(b, tok_cb, ve_cb):
        out_toks = []
        down_done[0] = max(down_done[0], pl.cnt["pe"])
        for jb in range(2):
            sl = slice(jb * JBW, (jb + 1) * JBW)
            pl.wait("t", "ve", pl.cnt["ve"])

            def wsrc(e):
                if e < 8:
                    return wg_d[e], wu_d[e], wd_d[e]
                h = e - 8
                return (swg_d[:, h * 1408:(h + 1) * 1408],
                        swu_d[:, h * 1408:(h + 1) * 1408],
                        swd_d[h * 1408:(h + 1) * 1408, :])

            def load_wchunk(e, f, gate):
                wgm, wum, _ = wsrc(e)
                par = f % 2
                return QG.batch([
                    (wgc[:, par], wgm[:, f * 128:(f + 1) * 128].rearrange(
                        "(c p) f -> p c f", p=128)),
                    (wuc[:, par], wum[:, f * 128:(f + 1) * 128].rearrange(
                        "(c p) f -> p c f", p=128)),
                ], gate=gate)

            def load_wdchunk(e, f, gate):
                _, _, wdm = wsrc(e)
                return QG.batch([(wdc[:, f, :],
                                  wdm[f * 128:(f + 1) * 128, :])], gate=gate)

            def load_cb(e, gate):
                return QY.batch([(cb_b[:, e % 2, :], bass.AP(
                    scrcb_d.tensor, e * S + jb * JBW, [[0, 128], [1, JBW]]))],
                    gate=gate)

            bid_w = {(0, 0): load_wchunk(0, 0, ("ve", pl.cnt["ve"])),
                     (0, 1): load_wchunk(0, 1, ("ve", pl.cnt["ve"]))}
            bid_cb = {0: load_cb(0, [("ve", pl.cnt["ve"]), tok_cb])}
            bid_wd = {}
            prev_sig = {}
            cb_used = {}
            ve_ttB = [0]
            ve_yacc = {}
            for e in range(NE):
                for f in range(NF):
                    par = f % 2
                    bid_wd[(e, f)] = load_wdchunk(e, f, ("pe", down_done[0]))
                    if e + 1 < 8 and f == 0 and (e + 1) not in bid_cb:
                        bid_cb[e + 1] = load_cb(
                            e + 1, ("ve", cb_used.get((e + 1) % 2, 0)))
                    QG.wait_tok("t", bid_w[(e, f)])
                    pl.wait("t", "se", prev_sig.get("A", 0))
                    for k in range(NK):
                        for h2 in range(2):
                            pe_g = t_mm(bank(h2), wgc[:, par, k, :],
                                        xln[:, k, jb * JBW + h2 * 512:
                                            jb * JBW + (h2 + 1) * 512],
                                        start=(k == 0), stop=(k == NK - 1),
                                        inc=(k == NK - 1 and h2 == 1))
                    pl.wait("t", "ve", ve_ttB[0])
                    for k in range(NK):
                        for h2 in range(2):
                            pe_u = t_mm(bank(2 + h2), wuc[:, par, k, :],
                                        xln[:, k, jb * JBW + h2 * 512:
                                            jb * JBW + (h2 + 1) * 512],
                                        start=(k == 0), stop=(k == NK - 1),
                                        inc=(k == NK - 1 and h2 == 1))
                    pl.wait("s", "pe", pe_g)
                    pl.wait("s", "ve", ve_ttB[0])
                    se_si = s_op(lambda E: E.activation(tmpS, bank(0, 2),
                                                        AF.Silu))
                    prev_sig["A"] = se_si
                    prev_sig[par] = se_si
                    pl.wait("v", "pe", pe_u)
                    if e < 8:
                        QY.wait_tok("v", bid_cb[e])
                        v_op(lambda E, e=e: E.tensor_tensor(
                            tmp2, bank(2, 2), cb_b[:, e % 2, :], ALU.mult),
                            inc=False)
                        pl.wait("v", "se", se_si)
                        ve_g = v_op(lambda E, f=f: E.tensor_tensor(
                            g_st[:, f, :], tmpS, tmp2, ALU.mult), dr=True)
                        cb_used[e % 2] = ve_g
                    else:
                        pl.wait("v", "se", se_si)
                        ve_g = v_op(lambda E, f=f: E.tensor_tensor(
                            g_st[:, f, :], tmpS, bank(2, 2), ALU.mult))
                    ve_ttB[0] = ve_g
                    if f + 2 < NF:
                        if (e, f + 2) not in bid_w:
                            bid_w[(e, f + 2)] = load_wchunk(e, f + 2,
                                                           ("ve", ve_g))
                    elif f == NF - 1 and e + 1 < NE:
                        for f2 in (0, 1):
                            bid_w[(e + 1, f2)] = load_wchunk(e + 1, f2,
                                                             ("ve", ve_g))
                pl.wait("t", "ve", ve_g)
                for k in range(NK):
                    db = 4 + 2 * (k % 2)
                    pl.wait("t", "ve", ve_yacc.get(db, 0))
                    for ff in (NF - 3, NF - 2, NF - 1):
                        QG.wait_tok("t", bid_wd[(e, ff)])
                    for f in range(NF):
                        for h2 in range(2):
                            pe_d = t_mm(bank(db + h2),
                                        wdc[:, f, k * 128:(k + 1) * 128],
                                        g_st[:, f, h2 * 512:(h2 + 1) * 512],
                                        start=(f == 0), stop=(f == NF - 1),
                                        inc=(f == NF - 1 and h2 == 1))
                    pl.wait("v", "pe", pe_d)
                    if e == 0:
                        ve_y = v_op(lambda E, k=k, db=db: E.tensor_copy(
                            yacc[:, k, :], bank(db, 2)))
                    else:
                        ve_y = v_op(lambda E, k=k, db=db: E.tensor_tensor(
                            yacc[:, k, :], yacc[:, k, :], bank(db, 2), ALU.add))
                    ve_yacc[db] = ve_y
                    down_done[0] = pe_d
            ve_f = None
            pl.drain("v")
            for k in range(NK):
                ve_f = v_op(lambda E, k=k, sl=sl: E.tensor_tensor(
                    xres[:, k, sl], xres[:, k, sl], yacc[:, k, :], ALU.add))
            out_toks.append(QY.batch(
                [(out_d[b].rearrange("(c p) s -> p c s", p=128)[:, :, sl],
                  xres[:, :, sl])], gate=("ve", ve_f)))
            if DEBUG:
                QY.batch([(dbg["x2"][b].rearrange("c p s -> p c s")[:, :, sl],
                           xres[:, :, sl])], gate=("ve", ve_f))
        return out_toks

    prev_out = []
    for b in range(NB):
        x_tok = QY.batch([(xres[:],
                           x_d[b].rearrange("(c p) s -> p c s", p=128))],
                         gate=prev_out if prev_out else None)
        ln_done, ln_tok_w = emit_ln(x_tok)
        if DEBUG:
            for k in range(NK):
                QY.batch([(dbg["xln"][b, 0, k], xln[:, k, :])],
                         gate=("ve", ln_done))
        uv_tok = QG.batch([(uT_s, ut_d.rearrange("(c p) q -> p c q", p=128)),
                           (vT_s, vt_d.rearrange("(c p) q -> p c q", p=128))],
                          gate=[("ve", ln_done), ln_tok_w])
        for p in range(NPAIR):
            emit_pair(p, uv_tok, ln_done)
            if DEBUG and p == 0:
                QY.batch([(dbg["qz"][b, 0], qzu_r[:]),
                          (dbg["qz"][b, 1], qzv_r[:])],
                         gate=("ve", max(proj_bank_ve.values())))
        last_pe = pl.cnt["pe"]
        uvn_tok = QG.batch([(u_s, u_d.rearrange("(c p) q -> p c q", p=128)),
                            (v_s, v_d.rearrange("(c p) q -> p c q", p=128))],
                           gate=("pe", last_pe))
        x2_tok = QY.batch([(xres[:],
                            x_d[b].rearrange("(c p) s -> p c s", p=128))],
                          gate=("pe", last_pe))
        if DEBUG:
            for qi in range(2):
                for c in range(NK):
                    QY.batch([(dbg["qhv"][b, qi, c], qhv[:, qi, c, :])],
                             gate=("ve", obank_ve[0]))
        fin_ve = emit_final(x2_tok, uvn_tok)
        pl.wait("t", "ve", fin_ve)
        pl.wait("s", "ve", fin_ve)
        ln2_done, _ = emit_ln(None)
        if DEBUG:
            for k in range(NK):
                QY.batch([(dbg["xln"][b, 1, k], xln[:, k, :])],
                         gate=("ve", ln2_done))
        tok_cb, ve_cb = emit_router(b, ln2_done)
        prev_out = emit_ffn(b, tok_cb, ve_cb)

    QG.flush()
    QY.flush()
    finh = sems["fin"]
    for eng in ENGS:
        pl.emit(eng, lambda E, h=finh: E.sem_inc(h, 1))
    pl.emit("g", lambda E, h=finh: E.wait_ge(h, 5))

    with ctx:
        with nc.Block() as block:
            @block.tensor
            def _(t):
                for fn in pl.q["t"]:
                    fn(t)

            @block.vector
            def _(v):
                for fn in pl.q["v"]:
                    fn(v)

            @block.scalar
            def _(s):
                for fn in pl.q["s"]:
                    fn(s)

            @block.sync
            def _(y):
                for fn in pl.q["y"]:
                    fn(y)

            @block.gpsimd
            def _(g):
                for fn in pl.q["g"]:
                    fn(g)
    return nc


_CACHE = {}


def kernel(L0_attn_qz, L0_attn_ternary_factor_u, L0_attn_ternary_factor_v,
           L0_ffn_gate_w, L0_ffn_w_gate, L0_ffn_w_up, L0_ffn_w_down,
           L0_ffn_sw_gate, L0_ffn_sw_up, L0_ffn_sw_down):
    x = np.ascontiguousarray(np.transpose(np.asarray(L0_attn_qz, np.float32),
                                          (0, 2, 1)))
    u = np.ascontiguousarray(np.asarray(L0_attn_ternary_factor_u, np.float32))
    v = np.ascontiguousarray(np.asarray(L0_attn_ternary_factor_v, np.float32))
    common = {
        "u": u, "v": v,
        "ut": np.ascontiguousarray(u.T),
        "vt": np.ascontiguousarray(v.T),
        "gwt": np.ascontiguousarray(np.asarray(L0_ffn_gate_w, np.float32).T),
        "ident": np.eye(128, dtype=np.float32),
        "wg": np.ascontiguousarray(np.asarray(L0_ffn_w_gate, np.float32)),
        "wu": np.ascontiguousarray(np.asarray(L0_ffn_w_up, np.float32)),
        "wd": np.ascontiguousarray(np.asarray(L0_ffn_w_down, np.float32)),
        "swg": np.ascontiguousarray(np.asarray(L0_ffn_sw_gate, np.float32)),
        "swu": np.ascontiguousarray(np.asarray(L0_ffn_sw_up, np.float32)),
        "swd": np.ascontiguousarray(np.asarray(L0_ffn_sw_down, np.float32)),
    }
    if "nc" not in _CACHE:
        _CACHE["nc"] = build_program()
    nc = _CACHE["nc"]
    in_maps = []
    for c in range(8):
        m = dict(common)
        m["x"] = np.ascontiguousarray(x[2 * c:2 * c + 2])
        in_maps.append(m)
    res = run_bass_kernel_spmd(nc, in_maps, core_ids=list(range(8)))
    out = np.empty((16, 1024, 2048), np.float32)
    for c in range(8):
        out[2 * c:2 * c + 2] = res.results[c]["out"]
    return np.ascontiguousarray(np.transpose(out, (0, 2, 1)))
